# version 03 (final)
# baseline (speedup 1.0000x reference)
import contextlib

import numpy as np

import concourse.bass as bass
import concourse.mybir as mybir
from concourse.bass_utils import run_bass_kernel_spmd

F32 = mybir.dt.float32
AF = mybir.ActivationFunctionType
ALU = mybir.AluOpType
AX = mybir.AxisListType

NCORES = 8


class Geo:
    def __init__(self, D=4096, SL=512, T=256):
        self.D = D
        self.SL = SL
        self.T = T
        self.NCH = D // 128
        self.H = D // 64
        self.HH = self.NCH
        self.VCH = 32
        self.TOK = 2 * SL
        self.NOUT = self.TOK // T
        self.HALF = T // 2
        assert self.HALF % 8 == 0 and self.HALF >= 32



def strips_from_wT(W_out_in, g):
    D_out, D_in = W_out_in.shape
    noc, nkc = D_out // 128, D_in // 128
    a = W_out_in.reshape(noc, 128, nkc, 128)
    return np.ascontiguousarray(a.transpose(0, 3, 2, 1)).reshape(noc, 128, nkc * 128)


def vperm_indices(g):
    idx = np.zeros(g.VCH * 128, np.int64)
    valid = np.zeros(g.VCH * 128, bool)
    for cv in range(g.VCH):
        for dp in range(2):
            for pg in range(2 * g.HH):
                gp, e = pg // g.HH, pg % g.HH
                gg = 2 * e + gp
                p = cv * 128 + dp * 64 + pg
                idx[p] = gg * 64 + (2 * cv + dp)
                valid[p] = True
    return idx, valid


def attperm_indices(g):
    idx = np.zeros(g.VCH * 128, np.int64)
    valid = np.zeros(g.VCH * 128, bool)
    for kc in range(g.VCH):
        for p in range(128):
            dp, u = p // 64, p % 64
            if u < 2 * g.HH:
                hp, c = u // g.HH, u % g.HH
                h = 2 * c + hp
                idx[kc * 128 + p] = h * 64 + (2 * kc + dp)
                valid[kc * 128 + p] = True
    return idx, valid


def prep_weights(P, g):
    f32 = np.float32
    W = {}
    D, NCH, VCH = g.D, g.NCH, g.VCH
    W["wq_s"] = strips_from_wT(P["L0_attn_Wq"].astype(f32), g)
    W["wk_s"] = strips_from_wT(P["L0_attn_Wk"].astype(f32), g)
    vidx, vval = vperm_indices(g)
    WvT_cols = np.zeros((VCH * 128, D), f32)
    WvT_cols[vval] = P["L0_attn_Wv"][vidx[vval]]
    W["wv_s"] = strips_from_wT(WvT_cols, g)
    aidx, aval = attperm_indices(g)
    WoT = np.zeros((VCH * 128, D), f32)
    WoT[aval] = np.ascontiguousarray(P["L0_attn_Wo"].T)[aidx[aval]]
    a = WoT.reshape(VCH, 128, D // 128, 128)
    W["wo_s"] = np.ascontiguousarray(a.transpose(2, 1, 0, 3)).reshape(D // 128, 128, VCH * 128)
    W["l0b_s"] = strips_from_wT(P["L0_ffn_base_W"].astype(f32), g)
    gw = P["L0_ffn_gate_W"].reshape(28, NCH, 128)
    W["gate_s"] = np.ascontiguousarray(gw.transpose(2, 1, 0)).reshape(128, NCH * 28)
    Af = P["L0_ffn_lora_A"].reshape(28 * 8, D)
    Ap = np.zeros((256, D), f32)
    Ap[:224] = Af
    W["la_s"] = strips_from_wT(Ap, g)
    Bf = np.ascontiguousarray(P["L0_ffn_lora_B"].transpose(0, 2, 1)).reshape(224, D)
    Bp = np.zeros((256, D), f32)
    Bp[:224] = Bf
    bb = Bp.reshape(2, 128, D // 128, 128)
    W["lb_s"] = np.ascontiguousarray(bb.transpose(2, 1, 0, 3)).reshape(D // 128, 128, 256)
    wd = P["L1_attn_w_down"].reshape(128, NCH, 128)
    W["wdn_s"] = np.ascontiguousarray(wd.transpose(2, 1, 0)).reshape(128, NCH * 128)
    iw = P["L1_attn_hy_in_w"]
    inw = np.zeros((128, 596), f32)
    inw[:, 0:512] = iw[0:512].T
    inw[:, 512:528] = iw[512:528].T
    inw[:, 544:548] = iw[544:548].T
    inw[:, 576:580] = iw[548:552].T
    inw[:, 580:596] = iw[528:544].T
    W["inw_s"] = inw
    fc = P["L1_attn_hy_fcD_w"]
    fcd = np.zeros((128, 8), f32)
    fcd[:, 0:4] = fc[:, 0:128].T
    fcd[:, 4:8] = fc[:, 128:256].T
    W["fcd_s"] = fcd
    ow = P["L1_attn_hy_out_w"]
    outw = np.zeros((128, 256), f32)
    outw[:, 0:128] = ow[:, 0:128].T
    outw[:, 128:256] = ow[:, 128:256].T
    W["outw_s"] = outw
    wu = P["L1_attn_w_up"].reshape(D // 128, 128, 128)
    W["wup_s"] = np.ascontiguousarray(wu.transpose(0, 2, 1)).copy()
    W["l1b_s"] = strips_from_wT(P["L1_ffn_base_w"].astype(f32), g)
    AR = np.concatenate([P["L1_ffn_router_w"], np.zeros((48, D), f32),
                         P["L1_ffn_A"]], 0)
    ar = AR.reshape(128, NCH, 128)
    W["ar_s"] = np.ascontiguousarray(ar.transpose(2, 1, 0)).reshape(128, NCH * 128)
    B1 = np.ascontiguousarray(P["L1_ffn_B"].transpose(0, 2, 1)).reshape(1024, D)
    b1 = B1.reshape(8, 128, D // 128, 128)
    W["bt_s"] = np.ascontiguousarray(b1.transpose(2, 1, 0, 3)).reshape(D // 128, 128, 1024)
    return W


def prep_vecs(P, g):
    f32 = np.float32
    NCH, VCH = g.NCH, g.VCH
    cols = []

    def col(v):
        cols.append(np.asarray(v, f32).reshape(128))
        return len(cols) - 1

    ix = {}
    p64 = np.arange(128) % 64
    ix["mq_mu"] = col(P["L0_attn_mu_q"][p64])
    ix["mq_sg"] = col(P["L0_attn_sigma_q"][p64])
    ix["mq_ep"] = col(P["L0_attn_eps_q"][p64])
    ix["mk_mu"] = col(P["L0_attn_mu_kv"][p64])
    ix["mk_sg"] = col(P["L0_attn_sigma_kv"][p64])
    ix["mk_ep"] = col(P["L0_attn_eps_kv"][p64])
    dtb = np.zeros(128, f32)
    dtb[32:36] = P["L1_attn_hy_dt_bias"]
    dtb[64:68] = P["L1_attn_hy_dt_bias"]
    ix["dtb"] = col(dtb)
    dv = np.zeros(128, f32)
    dv[32:36] = P["L1_attn_hy_D"]
    ix["hyD"] = col(dv)
    ix["bdown"] = col(P["L1_attn_b_down"])
    ix["gate1"] = col(np.full(128, float(P["L1_attn_gate1"]), f32))
    nw = P["L1_attn_hy_norm_w"]
    ix["nw0"] = col(nw[0:128])
    ix["nw1"] = col(nw[128:256])
    cw = P["L1_attn_hy_conv_w"][:, 0, :]
    cb = P["L1_attn_hy_conv_b"]
    for t, wcol in (("5", 5), ("6", 6)):
        for c in range(2):
            ix[f"cw{t}{c}"] = col(cw[c * 128:(c + 1) * 128, wcol])
        v = np.zeros(128, f32)
        v[0:16] = cw[256:272, wcol]
        ix[f"cw{t}4"] = col(v)
        v = np.zeros(128, f32)
        v[0:16] = cw[272:288, wcol]
        ix[f"cw{t}5"] = col(v)
    for c in range(2):
        ix[f"cb{c}"] = col(cb[c * 128:(c + 1) * 128])
    v = np.zeros(128, f32)
    v[0:16] = cb[256:272]
    ix["cb4"] = col(v)
    v = np.zeros(128, f32)
    v[0:16] = cb[272:288]
    ix["cb5"] = col(v)
    ix["bup0"] = len(cols)
    for c in range(NCH):
        col(P["L1_attn_b_up"][c * 128:(c + 1) * 128])
    ix["l1bb0"] = len(cols)
    for c in range(NCH):
        col(P["L1_ffn_base_b"][c * 128:(c + 1) * 128])
    dmap = lambda cv: (2 * cv + np.arange(128) // 64)
    ix["mv_mu0"] = len(cols)
    for cv in range(VCH):
        col(P["L0_attn_mu_kv"][dmap(cv) % len(P["L0_attn_mu_kv"])])
    ix["mv_sg0"] = len(cols)
    for cv in range(VCH):
        col(P["L0_attn_sigma_kv"][dmap(cv) % len(P["L0_attn_sigma_kv"])])
    ix["mv_ep0"] = len(cols)
    for cv in range(VCH):
        col(P["L0_attn_eps_kv"][dmap(cv) % len(P["L0_attn_eps_kv"])])
    return np.stack(cols, axis=1), ix


def prep_consts(g):
    f32 = np.float32
    C = {}
    C["ident"] = np.eye(128, dtype=f32)
    C["onescol"] = np.ones((128, 1), f32)
    C["onesrow"] = np.ones((1, 128), f32)
    eh = np.zeros((128, 256), f32)
    eh2 = np.zeros((128, 256), f32)
    for c in range(2):
        for p in range(128):
            h = 2 * c + p // 64
            eh[32 + h, c * 128 + p] = 1.0
            eh2[64 + h, c * 128 + p] = 1.0
    C["e_hyd"] = eh
    C["e_hyd2"] = eh2
    el = np.zeros((28, 256), f32)
    for c in range(2):
        for p in range(128):
            e = c * 16 + p // 8
            if e < 28:
                el[e, c * 128 + p] = 2.0
    C["e_lora"] = el
    e1 = np.zeros((16, 1024), f32)
    for c in range(8):
        for p in range(128):
            e1[2 * c + p // 64, c * 128 + p] = 0.5
    C["e_l1"] = e1
    dp = np.zeros((128, 128), f32)
    for r in range(64):
        dp[64 + r, r] = 1.0
        dp[64 + r, 64 + r] = 1.0
    C["dupc"] = dp
    o1p = np.zeros((128, 128), f32)
    o1p[0, :] = 1.0
    C["ones1p"] = o1p
    oc16 = np.zeros((128, 1), f32)
    oc16[0:16, 0] = 1.0
    C["onescol16"] = oc16
    return C


def prep_x(x, core, g):
    s0 = core * g.SL
    xs = np.asarray(x[:, s0:s0 + g.SL, :], np.float32)
    a = xs.reshape(2, g.NOUT, g.HALF, g.NCH, 128)
    a = a.transpose(1, 4, 3, 0, 2)
    return np.ascontiguousarray(a).reshape(g.NOUT, 128, g.NCH * g.T)


def unprep_x(y, g):
    a = y.reshape(g.NOUT, 128, g.NCH, 2, g.HALF)
    a = a.transpose(3, 0, 4, 2, 1)
    return np.ascontiguousarray(a).reshape(2, g.SL, g.D)



class Build:
    def __init__(self):
        self.cnt = {"W": 0, "G": 0, "T": 0, "V": 0, "S": 0}
        self.lastwait = {}
        self.sem = {}

    def inc(self, which, instr, n=None):
        n = n if n is not None else (16 if which in ("W", "G") else 1)
        self.cnt[which] += n
        instr.then_inc(self.sem[which], n)
        return self.cnt[which]

    def wait(self, eng, which, val):
        if val <= 0:
            return
        key = (id(eng), which)
        if self.lastwait.get(key, -1) >= val:
            return
        eng.wait_ge(self.sem[which], val)
        self.lastwait[key] = val


def build_full(g, NV, vix):
    nc = bass.Bass("TRN2", target_bir_lowering=False,
                   detect_race_conditions=False)
    D, NCH, VCH, T, HALF, H, HH = g.D, g.NCH, g.VCH, g.T, g.HALF, g.H, g.HH
    NOUT = g.NOUT
    mini = (HH < 32)
    NB = HALF // 8

    dr = {}
    for n, sh in [("xin", (NOUT, 128, NCH * T)),
                  ("wq_s", (NCH, 128, NCH * 128)), ("wk_s", (NCH, 128, NCH * 128)),
                  ("wv_s", (VCH, 128, NCH * 128)), ("wo_s", (NCH, 128, VCH * 128)),
                  ("l0b_s", (NCH, 128, NCH * 128)), ("gate_s", (128, NCH * 28)),
                  ("la_s", (2, 128, NCH * 128)), ("lb_s", (NCH, 128, 256)),
                  ("wdn_s", (128, NCH * 128)), ("inw_s", (128, 596)),
                  ("fcd_s", (128, 8)), ("outw_s", (128, 256)),
                  ("wup_s", (NCH, 128, 128)), ("l1b_s", (NCH, 128, NCH * 128)),
                  ("ar_s", (128, NCH * 128)), ("bt_s", (NCH, 128, 1024)),
                  ("ident", (128, 128)), ("onescol", (128, 1)), ("onesrow", (1, 128)),
                  ("e_hyd", (128, 256)), ("e_hyd2", (128, 256)), ("e_lora", (28, 256)), ("e_l1", (16, 1024)),
                  ("dupc", (128, 128)), ("ones1p", (128, 128)),
                  ("onescol16", (128, 1)), ("vecs", (128, NV))]:
        dr[n] = nc.dram_tensor(n, list(sh), F32, kind="ExternalInput")
    xout = nc.dram_tensor("xout", [NOUT, 128, NCH * T], F32, kind="ExternalOutput")
    dbg = getattr(g, "dbg", "")
    dbgt = {}
    if dbg:
        for nm in ("dbgA", "dbgB", "dbgC", "dbgX"):
            dbgt[nm] = nc.dram_tensor(nm, [128, VCH * T], F32, kind="ExternalOutput")

    es = contextlib.ExitStack()
    sb = lambda name, shape: es.enter_context(nc.sbuf_tensor(name, list(shape), F32))
    psf = lambda name, shape: es.enter_context(nc.psum_tensor(name, list(shape), F32))

    xbuf = sb("xbuf", (128, NCH, T))
    bufA = sb("bufA", (128, VCH, T))
    bufB = sb("bufB", (128, VCH, T))
    bufC = sb("bufC", (128, VCH, T))
    wbuf = [sb(f"wbuf{i}", (128, max(NCH, VCH) * 128)) for i in range(2)]
    w_sb = sb("w_sb", (128, 1, 8 * H))
    wT_sb = sb("wT_sb", (128, 2, 8, 128))
    at_sb = sb("at_sb", (64, 1, 8, 128))
    sq_sb = sb("sq_sb", (128, 2, T))
    mx_sb = sb("mx_sb", (128, 8))
    z_sb = sb("z_sb", (128, 8))
    zr_sb = sb("zr_sb", (128, 8))
    zt_sb = sb("zt_sb", (128, 8))
    st1 = sb("st1", (128, T))
    st2 = sb("st2", (128, T))
    st3 = sb("st3", (128, T))
    st4 = sb("st4", (128, T))
    ltm = sb("ltm", (128, 28))
    t8a = sb("t8a", (128, 8))
    t8b = sb("t8b", (128, 8))
    p1_sb = sb("p1_sb", (128, 4))
    gtm = sb("gtm", (128, 28))
    gfm = sb("gfm", (28, T))
    cfm = sb("cfm", (16, T))
    lg16 = sb("lg16", (128, 16))
    zx_sb = bufC[:, 0:6, :]
    h0_sb = bufC[:, 6, :]
    hc_sb = bufC[:, 7, :]
    hd_sb = bufC[:, 8, :]
    hy1 = bufC[:, 9:11, :]
    hy2 = bufC[:, 11:13, :]
    hy3 = bufC[:, 13:15, :]
    bc_sb = bufC[:, 15, :]
    rh_sb = bufC[:, 16, :]
    rhd_sb = bufC[:, 17, :]
    gate_w = sb("gate_w", (128, NCH * 28))
    inw_w = sb("inw_w", (128, 596))
    fcd_w = sb("fcd_w", (128, 8))
    outw_w = sb("outw_w", (128, 256))
    vecs = sb("vecs_sb", (128, NV))
    ident = sb("ident_sb", (128, 128))
    onescol = sb("onescol_sb", (128, 1))
    onesrow = sb("onesrow_sb", (1, 128))
    e_hyd = sb("e_hyd_sb", (128, 256))
    e_hyd2 = sb("e_hyd2_sb", (128, 256))
    e_lora = sb("e_lora_sb", (28, 256))
    e_l1 = sb("e_l1_sb", (16, 1024))
    dupc = sb("dupc_sb", (128, 128))
    ones1p = sb("ones1p_sb", (128, 128))
    onescol16 = sb("onescol16_sb", (128, 1))
    dbgS = sb("dbgS", (128, 512)) if getattr(g, "dbg", "") else None
    mqv = sb("mqv", (128, 1))
    mkv = sb("mkv", (128, 1))
    mvv = sb("mvv", (128, VCH))
    alph = sb("alph", (128, 1))
    alph1 = sb("alph1", (128, 1))

    pb = [psf(f"pb{i}", (128, 512)) for i in range(8)]

    semW = es.enter_context(nc.semaphore("semW"))
    semG = es.enter_context(nc.semaphore("semG"))
    semT = es.enter_context(nc.semaphore("semT"))
    semV = es.enter_context(nc.semaphore("semV"))
    semS = es.enter_context(nc.semaphore("semS"))
    block = es.enter_context(nc.Block())

    b = Build()
    b.sem = {"W": semW, "G": semG, "T": semT, "V": semV, "S": semS}
    te, ve, se, sy, gp = nc.tensor, nc.vector, nc.scalar, nc.sync, nc.gpsimd
    W, G, Tm, V, S = "W", "G", "T", "V", "S"

    vcol = lambda name, off=0: vecs[:, vix[name] + off: vix[name] + off + 1]

    _last_te = [None]

    def temm(*a, **k):
        _last_te[0] = te.matmul(*a, **k)
        return _last_te[0]

    def tetr(*a, **k):
        _last_te[0] = te.transpose(*a, **k)
        return _last_te[0]

    def tinc():
        return b.inc("T", _last_te[0])

    def vsync(instr):
        c = b.inc("V", instr)
        b.lastwait.pop((id(ve), "V"), None)
        ve.wait_ge(semV, c)
        b.lastwait[(id(ve), "V")] = c
        return c

    for name, dst in [("gate_s", gate_w), ("inw_s", inw_w), ("fcd_s", fcd_w),
                      ("outw_s", outw_w), ("vecs", vecs), ("ident", ident),
                      ("onescol", onescol), ("onesrow", onesrow), ("e_hyd", e_hyd), ("e_hyd2", e_hyd2),
                      ("e_lora", e_lora), ("e_l1", e_l1), ("dupc", dupc),
                      ("ones1p", ones1p), ("onescol16", onescol16)]:
        b.inc(W, sy.dma_start(dst[:], dr[name][:, :]))
    cW0 = b.cnt["W"]
    b.wait(ve, W, cW0)
    vsync0 = None
    ve.tensor_scalar(mqv[:], vcol("mq_ep"), vcol("mq_sg"), vcol("mq_mu"),
                     op0=ALU.mult, op1=ALU.add)
    mkV = b.inc(V, ve.tensor_scalar(mkv[:], vcol("mk_ep"), vcol("mk_sg"),
                                    vcol("mk_mu"), op0=ALU.mult, op1=ALU.add))
    b.wait(se, V, mkV)
    se.activation(mqv[:], mqv[:], AF.Sigmoid)
    cS0 = b.inc(S, se.activation(mkv[:], mkv[:], AF.Sigmoid))
    b.wait(ve, S, cS0)
    ve.tensor_scalar(mqv[:], mqv[:], 0.125, None, op0=ALU.mult)
    ve.tensor_tensor(mvv[:], vecs[:, vix["mv_ep0"]:vix["mv_ep0"] + VCH],
                     vecs[:, vix["mv_sg0"]:vix["mv_sg0"] + VCH], op=ALU.mult)
    cV0 = b.inc(V, ve.tensor_tensor(mvv[:], mvv[:],
                                    vecs[:, vix["mv_mu0"]:vix["mv_mu0"] + VCH],
                                    op=ALU.add))
    b.wait(se, V, cV0)
    se.activation(mvv[:], mvv[:], AF.Sigmoid)
    cS1 = b.inc(S, se.activation(alph[:], vcol("gate1"), AF.Sigmoid))
    b.wait(ve, S, cS1)
    cV1 = b.inc(V, ve.tensor_scalar(alph1[:], alph[:], -1.0, 1.0, op0=ALU.mult,
                                    op1=ALU.add))
    ve.memset(bufC[:, :, :], 0.0)
    ve.memset(wT_sb[:], 0.0)
    ve.memset(st1[:, :], 0.0)
    ve.memset(st2[:, :], 0.0)
    ve.memset(st3[:, :], 0.0)
    ve.memset(st4[:, :], 0.0)
    if mini:
        ve.memset(bufA[:, :, :], 0.0)
        ve.memset(bufB[:, :, :], 0.0)
        ve.memset(w_sb[:], 0.0)
        ve.memset(at_sb[:], 0.0)
        ve.memset(pb[4][:, :], 0.0)
        ve.memset(pb[5][:, :], 0.0)
        ve.memset(pb[6][:, :], 0.0)
        ve.memset(pb[7][:, :], 0.0)
    cV1 = b.inc(V, ve.engine_nop())

    prev_group_end = [0, 0]
    prev_evac = [cV1, cV1]

    def stream_chain(strips_dr, n_oc, n_kc, rhs_of, evac, n_strip_cols=None,
                     m=128, outpart=0):
        nsc = n_strip_cols or (n_kc * 128)
        cols_per_kc = nsc // n_kc
        for i in range(n_oc):
            par = i % 2
            b.wait(sy, Tm, prev_group_end[par])
            sd = b.inc(W, sy.dma_start(wbuf[par][:, 0:nsc], strips_dr[i]))
            b.wait(te, W, sd)
            b.wait(te, V, prev_evac[par])
            for kc in range(n_kc):
                temm(pb[par][outpart:outpart + m, 0:T],
                          wbuf[par][0:128, kc * cols_per_kc: kc * cols_per_kc + m],
                          rhs_of(kc), start=(kc == 0), stop=(kc == n_kc - 1))
            prev_group_end[par] = tinc()
            b.wait(ve, Tm, prev_group_end[par])
            prev_evac[par] = evac(i, pb[par][outpart:outpart + m, 0:T])

    xv = lambda c: xbuf[:, c, :]

    def dump(tag, **tiles):
        if dbg != tag:
            return
        b.wait(gp, V, b.cnt["V"])
        b.wait(gp, S, b.cnt["S"])
        b.wait(gp, Tm, b.cnt["T"])
        for nm, tile in tiles.items():
            n_el = 1
            for d_, cnt_ in enumerate(tile.shape[1:]):
                n_el *= cnt_
            b.inc(G, gp.dma_start(
                dbgt[nm][0:tile.shape[0], 0:n_el],
                tile.rearrange("p a x -> p (a x)") if len(tile.shape) == 3 else tile))
        b.wait(ve, G, b.cnt["G"])
        b.wait(se, G, b.cnt["G"])
        b.wait(te, G, b.cnt["G"])
        b.wait(sy, G, b.cnt["G"])

    LIM = getattr(g, "stage", 99)
    xstore_done = 0
    for ot in range(NOUT if LIM >= 99 else 1):
        b.wait(gp, G, xstore_done)
        xload = b.inc(G, gp.dma_start(xbuf[:, :, :], dr["xin"][ot]))
        b.wait(te, G, xload)

        def mk_evac(dst, maskv=None, per_chunk_mask=False):
            def evac(oc, pap):
                mv = mvv[:, oc:oc + 1] if per_chunk_mask else maskv
                return b.inc(V, ve.tensor_scalar(dst[:, oc, :], pap, mv, None,
                                                 op0=ALU.mult))
            return evac
        if LIM < 1:
            b.wait(gp, G, xload)
            xstore_done = b.inc(G, gp.dma_start(xout[ot], xbuf[:, :, :]))
            continue
        stream_chain(dr["wq_s"], NCH, NCH, xv, mk_evac(bufA, mqv[:, 0:1]))
        stream_chain(dr["wk_s"], NCH, NCH, xv, mk_evac(bufB, mkv[:, 0:1]))
        stream_chain(dr["wv_s"], VCH, NCH, xv, mk_evac(bufC, per_chunk_mask=True))
        qkv_done = b.cnt["V"]
        if ot == 0:
            dump("qkv", dbgA=bufA[:, :, :], dbgB=bufB[:, :, :], dbgC=bufC[:, :, :])

        if mini:
            b.wait(ve, V, qkv_done)
            ve.memset(pb[0][:, :], 0.0)
            ve.memset(pb[1][:, :], 0.0)
            qkv_done = b.inc(V, ve.engine_nop())

        if LIM < 2:
            b.wait(gp, V, b.cnt["V"])
            xstore_done = b.inc(G, gp.dma_start(xout[ot], xbuf[:, :, :]))
            continue
        kv_t = [wbuf[0][:, 0:NCH * 64].rearrange("p (c x) -> p c x", c=NCH),
                wbuf[1][:, 0:NCH * 64].rearrange("p (c x) -> p c x", c=NCH),
                wbuf[0][:, NCH * 64:NCH * 128].rearrange("p (c x) -> p c x", c=NCH),
                wbuf[1][:, NCH * 64:NCH * 128].rearrange("p (c x) -> p c x", c=NCH)]
        kvar_of = {(0, 0): 0, (1, 1): 1, (0, 1): 2, (1, 0): 3}
        qkv_T = max(prev_group_end)
        b.wait(ve, Tm, qkv_T)
        ve.memset(wbuf[0][0:64, NCH * 64:NCH * 128], 0.0)
        ve.memset(wbuf[0][64:128, NCH * 64:NCH * 128], 0.0)
        ve.memset(wbuf[0][64:128, 0:NCH * 64], 0.0)
        ve.memset(wbuf[1][0:64, 0:NCH * 64], 0.0)
        ve.memset(wbuf[1][0:64, NCH * 64:NCH * 128], 0.0)
        ve.memset(wbuf[1][64:128, NCH * 64:NCH * 128], 0.0)
        kz_done = b.inc(V, ve.engine_nop())
        kvar_done = 0
        last_scores_read = {0: qkv_done, 1: qkv_done}
        last_scores_T = 0
        last_wT_evac = {2: 0, 3: 0}
        last_at_evac = {4: 0, 5: 0}
        last_atr_evac = {6: 0, 7: 0}
        wsb_freeT = [0, 0]
        wT_dup_done = 0
        last_attn_T = 0
        for bi in range(min(NB, getattr(g, 'nbmax', 999))):
            j0 = bi * 8
            sub = j0 // 32
            par = 0
            if (bi % 4) == 0:
                c0 = sub * 32
                b.wait(sy, V, max(qkv_done, kz_done))
                b.wait(sy, Tm, last_scores_T)
                for (vi, dl, sl) in ((0, 0, 0), (1, 64, 64), (2, 0, 64), (3, 64, 0)):
                    sy.dma_start(kv_t[vi][dl:dl + 64, 0:NCH, 0:32],
                                 bufB[sl:sl + 64, 0:NCH, c0:c0 + 32]
                                 ).then_inc(semW, 16)
                    sy.dma_start(kv_t[vi][dl:dl + 64, 0:NCH, 32:64],
                                 bufB[sl:sl + 64, 0:NCH, HALF + c0:HALF + c0 + 32]
                                 ).then_inc(semW, 16)
                    b.cnt["W"] += 32
                kvar_done = b.cnt["W"]
            sbank = bi % 2
            b.wait(te, W, kvar_done)
            b.wait(te, V, last_scores_read[sbank])
            scps = pb[sbank]
            for s in range(8):
                j = j0 + s
                ks0 = (j - sub * 32)
                for t in range(2):
                    n = t * HALF + j
                    qv = bufA[:, 0:NCH, n]
                    for hp in range(2):
                        for gpp in range(2):
                            kv_ = kv_t[kvar_of[(hp, gpp)]][:, 0:NCH, ks0 + t * 32]
                            rowb = hp * 64 + t * 32
                            temm(scps[rowb:rowb + HH,
                                      s * H + gpp * HH: s * H + gpp * HH + HH],
                                 qv, kv_, start=True, stop=True,
                                 tile_position=(0, rowb))
            last_scores_T = tinc()
            if dbg == "attn_sc" and ot == 0 and bi == 0:
                b.wait(ve, Tm, last_scores_T)
                b.inc(V, ve.tensor_copy(dbgS[:, :], scps[:, 0:512]))
                dump("attn_sc", dbgA=dbgS[:, :])
            if getattr(g, "sub", 9) < 2:
                continue
            b.wait(ve, Tm, last_scores_T)
            b.wait(ve, Tm, wsb_freeT[par])
            sc3 = scps[:, 0:8 * H].rearrange("p (s x) -> p s x", s=8)
            vsync(ve.tensor_reduce(mx_sb[:], sc3, op=ALU.max, axis=AX.X))
            wv_ = w_sb[:, par, :]
            subV = b.inc(V, ve.tensor_tensor(
                wv_.rearrange("p (s x) -> p s x", s=8), sc3,
                mx_sb[:].unsqueeze(2).broadcast_to((128, 8, H)),
                op=ALU.subtract))
            last_scores_read[sbank] = subV
            if dbg == "attn_sm" and ot == 0 and bi == 0:
                dump("attn_sm", dbgA=w_sb[:, 0, :])
            b.wait(se, V, subV)
            expS = b.inc(S, se.activation(wv_, wv_, AF.Exp))
            b.wait(ve, S, expS)
            if dbg == "attn_sm" and ot == 0 and bi == 0:
                dump("attn_sm", dbgB=w_sb[:, 0, :])
            vsync(ve.tensor_reduce(z_sb[:], wv_.rearrange("p (s x) -> p s x", s=8),
                                   op=ALU.add, axis=AX.X))
            vsync(ve.reciprocal(zr_sb[:], z_sb[:]))
            vsync(ve.tensor_tensor(zt_sb[:], z_sb[:], zr_sb[:], op=ALU.mult))
            vsync(ve.tensor_scalar(zt_sb[:], zt_sb[:], -1.0, 2.0, op0=ALU.mult, op1=ALU.add))
            zrV = vsync(ve.tensor_tensor(zr_sb[:], zr_sb[:], zt_sb[:], op=ALU.mult))
            if dbg == "attn_sm" and ot == 0 and bi == 0:
                dump("attn_sm", dbgC=z_sb[:], dbgX=zr_sb[:])
            wnorm = b.inc(V, ve.tensor_tensor(
                wv_.rearrange("p (s x) -> p s x", s=8),
                wv_.rearrange("p (s x) -> p s x", s=8),
                zr_sb[:].unsqueeze(2).broadcast_to((128, 8, H)),
                op=ALU.mult))
            if ot == 0 and bi == 0:
                dump("attn_w", dbgA=w_sb[:, 0, :])
            if getattr(g, "sub", 9) < 3:
                continue
            b.wait(te, V, wnorm)
            for gsub in range(2):
                bank = 2 + gsub
                b.wait(te, V, last_wT_evac[bank])
                for s4 in range(4):
                    s = gsub * 4 + s4
                    inw = wv_[:, s * H:(s + 1) * H]
                    tetr(pb[bank][0:H, s4 * 128:s4 * 128 + 128], inw,
                         ident[:, 0:128])
                tT = tinc()
                b.wait(ve, Tm, tT)
                ev = b.inc(V, ve.tensor_copy(
                    wT_sb[0:H, 0, gsub * 4:gsub * 4 + 4, :]
                    .rearrange("p a x -> p (a x)"), pb[bank][0:H, 0:512]))
                last_wT_evac[bank] = ev
            wsb_freeT[par] = b.cnt["T"]
            b.wait(sy, V, max(last_wT_evac[2], last_wT_evac[3]))
            b.wait(sy, Tm, last_attn_T)
            sy.dma_start(wT_sb[64:128, 1, :, :], wT_sb[0:64, 0, :, :]
                         ).then_inc(semW, 16)
            b.cnt["W"] += 16
            wT_dup_done = b.cnt["W"]
            if ot == 0 and bi == 0:
                dump("attn_wt", dbgA=wT_sb[:, 0, :, :].rearrange("p a x -> p (a x)"))
            if getattr(g, "sub", 9) < 4:
                continue
            for gsub in range(2):
                bank = 4 + gsub
                b.wait(te, W, wT_dup_done)
                b.wait(te, V, last_at_evac[bank])
                for s4 in range(4):
                    s = gsub * 4 + s4
                    for t in range(2):
                        n = t * HALF + (j0 + s)
                        lhs = bufC[:, 0:VCH, n]
                        for dp in range(2):
                            rhs = wT_sb[:, dp, s, :] \
                                .rearrange("p (h x) -> p h x", h=2)[:, :, t * 32:t * 32 + HH]
                            temm(pb[bank][t * 32:t * 32 + VCH,
                                          s4 * 128 + dp * 64:
                                          s4 * 128 + dp * 64 + 2 * HH],
                                 lhs, rhs, start=True, stop=True)
                aT = tinc()
                last_attn_T = aT
                b.wait(ve, Tm, aT)
                if mini:
                    for t in range(2):
                        ev = b.inc(V, ve.tensor_copy(
                            at_sb[t * 32:t * 32 + VCH, par, gsub * 4:gsub * 4 + 4, :]
                            .rearrange("p a x -> p (a x)"),
                            pb[bank][t * 32:t * 32 + VCH, 0:512]))
                else:
                    ev = b.inc(V, ve.tensor_copy(
                        at_sb[:, par, gsub * 4:gsub * 4 + 4, :]
                        .rearrange("p a x -> p (a x)"), pb[bank][0:64, 0:512]))
                last_at_evac[bank] = ev
            if ot == 0 and bi == 0:
                dump("attn_at", dbgA=at_sb[0:64, 0, :, :].rearrange("p a x -> p (a x)"))
            if getattr(g, "sub", 9) < 5:
                continue
            bank = 6 + (bi % 2)
            b.wait(te, V, max(last_at_evac[4], last_at_evac[5]))
            b.wait(te, V, last_atr_evac[bank])
            for s in range(8):
                tetr(pb[bank][:, s * 64:s * 64 + 64],
                     at_sb[0:64, par, s, :], ident[0:64, 0:64])
            rT = tinc()
            b.wait(ve, Tm, rT)
            if dbg == "attn_atr" and ot == 0 and bi == 0:
                b.inc(V, ve.tensor_copy(dbgS[:, :], pb[bank][:, 0:512]))
                dump("attn_atr", dbgA=dbgS[:, :])
            dstap = bufA[:, :, :].rearrange("p c (t j) -> p c t j", t=2)[:, :, :, j0:j0 + 8]
            srcap = pb[bank][:, 0:512].rearrange("p (s t c) -> p c t s", s=8, t=2)
            last_atr_evac[bank] = b.inc(V, ve.tensor_copy(dstap, srcap))
            if dbg == "attn_b0" and ot == 0 and bi == 0:
                b.wait(ve, V, b.cnt["V"])
                b.inc(V, ve.tensor_copy(dbgS[:, :], pb[bank][:, 0:512]))
                dump("attn_b0", dbgA=bufA[:, :, :], dbgB=dbgS[:, :])
            if dbg == "attn_b0" and ot == 0 and bi == 0:
                break
        prev_group_end[0] = b.cnt["T"]
        prev_group_end[1] = b.cnt["T"]
        attn_done_V = b.cnt["V"]
        if mini and getattr(g, 'nbmax', 999) < NB:
            pass
        if ot == 0:
            dump("attn", dbgA=bufA[:, :, :])

        if LIM < 3:
            b.wait(gp, V, b.cnt["V"])
            xstore_done = b.inc(G, gp.dma_start(xout[ot], xbuf[:, :, :]))
            continue
        def oevac(oc, pap):
            return b.inc(V, ve.tensor_tensor(xbuf[:, oc, :], xbuf[:, oc, :],
                                             pap, op=ALU.add))
        b.wait(te, V, attn_done_V)
        stream_chain(dr["wo_s"], NCH, VCH, lambda kc: bufA[:, kc, :], oevac,
                     n_strip_cols=VCH * 128)

        def layer_norm(buf):
            b.wait(se, V, b.cnt["V"])
            b.wait(te, V, b.cnt["V"])
            sq_done = [0, 0]
            mm_done = [0, 0]
            for c in range(NCH):
                pr = c % 2
                b.wait(se, Tm, mm_done[pr])
                sq_done[pr] = b.inc(S, se.square(sq_sb[:, pr, :], buf[:, c, :]))
                b.wait(te, S, sq_done[pr])
                temm(pb[6][0:1, 0:T], onescol[:, 0:1], buf[:, c, :],
                          start=(c == 0), stop=(c == NCH - 1))
                temm(pb[7][0:1, 0:T], onescol[:, 0:1], sq_sb[:, pr, :],
                          start=(c == 0), stop=(c == NCH - 1))
                mm_done[pr] = tinc()
            b.wait(ve, Tm, b.cnt["T"])
            vsync(ve.tensor_scalar(st1[0:1, :], pb[6][0:1, 0:T], 1.0 / D, None,
                                   op0=ALU.mult))
            vsync(ve.tensor_scalar(st2[0:1, :], pb[7][0:1, 0:T], 1.0 / D, None,
                                   op0=ALU.mult))
            vsync(ve.tensor_tensor(st3[0:1, :], st1[0:1, :], st1[0:1, :],
                                   op=ALU.mult))
            vsync(ve.tensor_tensor(st2[0:1, :], st2[0:1, :], st3[0:1, :],
                                   op=ALU.subtract))
            statsV = vsync(ve.tensor_scalar(st2[0:1, :], st2[0:1, :], 1e-5,
                                            None, op0=ALU.add))
            b.wait(se, V, statsV)
            sqrtS = b.inc(S, se.activation(st3[0:1, :], st2[0:1, :], AF.Sqrt))
            b.wait(ve, S, sqrtS)
            vsync(ve.reciprocal(st4[0:1, :], st3[0:1, :]))
            vsync(ve.tensor_tensor(st3[0:1, :], st4[0:1, :], st4[0:1, :],
                                   op=ALU.mult))
            vsync(ve.tensor_tensor(st3[0:1, :], st3[0:1, :], st2[0:1, :],
                                   op=ALU.mult))
            vsync(ve.tensor_scalar(st3[0:1, :], st3[0:1, :], -0.5, 1.5,
                                   op0=ALU.mult, op1=ALU.add))
            rdone = vsync(ve.tensor_tensor(st4[0:1, :], st4[0:1, :],
                                           st3[0:1, :], op=ALU.mult))
            b.wait(te, V, rdone)
            temm(pb[6][:, 0:T], ones1p[:, 0:128], st1[0:128, :], start=True,
                 stop=True)
            temm(pb[6][:, T:2 * T], ones1p[:, 0:128], st4[0:128, :], start=True,
                 stop=True)
            bcT = tinc()
            b.wait(ve, Tm, bcT)
            mb = pb[6][:, 0:T].unsqueeze(1).broadcast_to((128, NCH, T))
            rb = pb[6][:, T:2 * T].unsqueeze(1).broadcast_to((128, NCH, T))
            bv = buf[:, 0:NCH, :]
            ve.tensor_tensor(bv, bv, mb, op=ALU.subtract)
            return b.inc(V, ve.tensor_tensor(bv, bv, rb, op=ALU.mult))

        if ot == 0:
            dump("oproj", dbgX=xbuf[:, :, :])
        ln1 = layer_norm(xbuf)
        if ot == 0:
            dump("ln1", dbgX=xbuf[:, :, :])

        if LIM < 4:
            b.wait(gp, V, b.cnt["V"])
            xstore_done = b.inc(G, gp.dma_start(xout[ot], xbuf[:, :, :]))
            continue
        b.wait(te, V, ln1)
        b.wait(te, W, cW0)
        for kc in range(NCH):
            temm(pb[6][0:28, 0:T], gate_w[:, kc * 28:kc * 28 + 28], xv(kc),
                      start=(kc == 0), stop=(kc == NCH - 1))
        lgT = tinc()
        nblk = max(T // 128, 1)
        blkw = min(T, 128)
        b.wait(ve, Tm, lgT)
        lgV = b.inc(V, ve.tensor_copy(bc_sb[0:28, 0:T], pb[6][0:28, 0:T]))
        b.wait(te, V, lgV)
        for blk in range(nblk):
            tetr(pb[7][0:blkw, blk * 28:blk * 28 + 28],
                         bc_sb[0:28, blk * blkw:(blk + 1) * blkw],
                         ident[0:28, 0:28])
        ltT = tinc()
        b.wait(ve, Tm, ltT)
        gfm_done = 0
        for blk in range(nblk):
            ltmv = ltm[0:blkw, :]
            vsync(ve.tensor_copy(ltmv, pb[7][0:blkw, blk * 28:blk * 28 + 28]))
            vsync(ve.max(t8a[0:blkw, :], ltmv))
            sdV = vsync(ve.tensor_tensor(p1_sb[0:blkw, 0:1], t8a[0:blkw, 0:1],
                                         t8a[0:blkw, 1:2], op=ALU.subtract))
            b.wait(se, V, sdV)
            sigS = b.inc(S, se.activation(p1_sb[0:blkw, 1:2], p1_sb[0:blkw, 0:1],
                                          AF.Sigmoid))
            b.wait(ve, S, sigS)
            vsync(ve.tensor_scalar(p1_sb[0:blkw, 2:3], p1_sb[0:blkw, 1:2], -1.0,
                                   1.0, op0=ALU.mult, op1=ALU.add))
            vsync(ve.tensor_scalar(gtm[0:blkw, :], ltmv, t8a[0:blkw, 0:1],
                                   p1_sb[0:blkw, 1:2], op0=ALU.is_equal,
                                   op1=ALU.mult))
            vsync(ve.tensor_scalar(ltm[0:blkw, :], ltmv, t8a[0:blkw, 1:2],
                                   p1_sb[0:blkw, 2:3], op0=ALU.is_equal,
                                   op1=ALU.mult))
            gV = vsync(ve.tensor_tensor(gtm[0:blkw, :], gtm[0:blkw, :],
                                        ltm[0:blkw, :], op=ALU.add))
            b.wait(te, V, gV)
            tetr(pb[7][0:28, 256 + blk * blkw: 256 + (blk + 1) * blkw],
                         gtm[0:blkw, 0:28], ident[0:blkw, 0:blkw])
            gtT = tinc()
            b.wait(ve, Tm, gtT)
            gfm_done = b.inc(V, ve.tensor_copy(
                gfm[0:28, blk * blkw:(blk + 1) * blkw],
                pb[7][0:28, 256 + blk * blkw:256 + (blk + 1) * blkw]))

        def bevac(oc, pap):
            return b.inc(V, ve.tensor_tensor(bufA[:, oc, :], xbuf[:, oc, :], pap,
                                             op=ALU.add))
        stream_chain(dr["l0b_s"], NCH, NCH, xv, bevac)

        def haevac(oc, pap):
            return b.inc(V, ve.tensor_copy(bufB[:, oc, :], pap))
        stream_chain(dr["la_s"], 2, NCH, xv, haevac)
        hV = b.cnt["V"]
        b.wait(te, V, max(gfm_done, hV))
        for c in range(2):
            temm(pb[2 + c][0:128, 0:T], e_lora[0:28, c * 128:(c + 1) * 128],
                      gfm[0:28, 0:T], start=True, stop=True)
        geT = tinc()
        b.wait(ve, Tm, geT)
        htV = 0
        for c in range(2):
            htV = b.inc(V, ve.tensor_tensor(bufB[:, c, :], bufB[:, c, :],
                                            pb[2 + c][0:128, 0:T], op=ALU.mult))

        def levac(oc, pap):
            return b.inc(V, ve.tensor_tensor(bufA[:, oc, :], bufA[:, oc, :], pap,
                                             op=ALU.add))
        b.wait(te, V, htV)
        stream_chain(dr["lb_s"], NCH, 2, lambda kc: bufB[:, kc, :], levac,
                     n_strip_cols=256)
        if ot == 0:
            dump("ffn0", dbgA=bufA[:, :, :], dbgB=gfm[0:28, :])
        cpV = b.inc(V, ve.tensor_copy(xbuf[:, 0:NCH, :], bufA[:, 0:NCH, :]))
        b.wait(te, V, cpV)
        ln2 = layer_norm(xbuf)
        if ot == 0:
            dump("ln2", dbgX=xbuf[:, :, :])

        if LIM < 5:
            b.wait(gp, V, b.cnt["V"])
            xstore_done = b.inc(G, gp.dma_start(xout[ot], xbuf[:, :, :]))
            continue
        b.wait(te, V, ln2)
        par = 0
        b.wait(sy, Tm, prev_group_end[par])
        sd = b.inc(W, sy.dma_start(wbuf[par][:, 0:NCH * 128], dr["wdn_s"][:, :]))
        b.wait(te, W, sd)
        b.wait(te, V, prev_evac[par])
        for kc in range(NCH):
            temm(pb[par][:, 0:T], wbuf[par][:, kc * 128:kc * 128 + 128],
                      xv(kc), start=(kc == 0), stop=(kc == NCH - 1))
        prev_group_end[par] = tinc()
        b.wait(se, Tm, prev_group_end[par])
        g0S = b.inc(S, se.activation(h0_sb[:], pb[par][:, 0:T], AF.Gelu,
                                     bias=vcol("bdown"), scale=1.0))
        b.wait(ve, S, g0S)
        h0V = b.inc(V, ve.tensor_copy(hc_sb[:], h0_sb[:]))
        prev_evac[par] = h0V

        def hydra_pass(uin, uout):
            b.wait(te, V, b.cnt["V"])
            mparts = [128, 128, 128, 128, 68, 16]
            banks = [(0, 0), (0, T), (1, 0), (1, T), (7, 0), (7, T)]
            coledges = [0, 128, 256, 384, 512, 580, 596]
            for ci in range(6):
                bk, off = banks[ci]
                temm(pb[bk][0:mparts[ci], off:off + T],
                          inw_w[:, coledges[ci]:coledges[ci + 1]], uin[:],
                          start=True, stop=True)
            ipT = tinc()
            b.wait(ve, Tm, ipT)
            ve.tensor_copy(zx_sb[:, 0, :], pb[0][:, 0:T])
            ve.tensor_copy(zx_sb[:, 1, :], pb[0][:, T:2 * T])
            ve.tensor_copy(zx_sb[:, 2, :], pb[1][:, 0:T])
            ve.tensor_copy(zx_sb[:, 3, :], pb[1][:, T:2 * T])
            ve.tensor_copy(zx_sb[0:68, 4, :], pb[7][0:68, 0:T])
            zxV = b.inc(V, ve.tensor_copy(zx_sb[0:16, 5, :], pb[7][0:16, T:2 * T]))
            b.wait(se, V, zxV)
            se.activation(hy1[32:36, 0, 0:HALF], zx_sb[32:36, 4, 0:HALF], AF.Exp,
                          bias=vecs[32:36, vix["dtb"]:vix["dtb"] + 1], scale=1.0)
            expP = b.inc(S, se.activation(hy1[64:68, 0, HALF:T],
                                          zx_sb[64:68, 4, HALF:T], AF.Exp,
                                          bias=vecs[64:68, vix["dtb"]:vix["dtb"] + 1],
                                          scale=1.0))
            b.wait(se, S, expP)
            se.activation(hy1[32:36, 0, 0:HALF], hy1[32:36, 0, 0:HALF], AF.Ln,
                          bias=1.0, scale=1.0)
            sp2 = b.inc(S, se.activation(hy1[64:68, 0, HALF:T],
                                         hy1[64:68, 0, HALF:T], AF.Ln,
                                         bias=1.0, scale=1.0))
            conv_list = [(2, 128, "cw50", "cw60", "cb0"),
                         (3, 128, "cw51", "cw61", "cb1"),
                         (4, 16, "cw54", "cw64", "cb4"),
                         (5, 16, "cw55", "cw65", "cb5")]
            for (ci, rows, n5, n6, nb) in conv_list:
                zc = zx_sb[0:rows, ci, :]
                l0 = zx_sb[0:rows, ci, 0:HALF]
                l1s = zx_sb[0:rows, ci, HALF:T]
                w5v = vecs[0:rows, vix[n5]:vix[n5] + 1]
                w6v = vecs[0:rows, vix[n6]:vix[n6] + 1]
                bvv = vecs[0:rows, vix[nb]:vix[nb] + 1]
                ve.tensor_scalar(hy2[0:rows, 0, 0:HALF], l1s, w6v, None,
                                 op0=ALU.mult)
                ve.scalar_tensor_tensor(l1s, l0, w5v, hy2[0:rows, 0, 0:HALF],
                                        op0=ALU.mult, op1=ALU.add)
                convV = b.inc(V, ve.tensor_scalar(l0, l0, w6v, None, op0=ALU.mult))
                b.wait(se, V, convV)
                b.inc(S, se.activation(zc, zc, AF.Silu, bias=bvv, scale=1.0))
            silu_done = b.cnt["S"]
            b.wait(ve, S, silu_done)
            bcV = b.inc(V, ve.tensor_tensor(hy2[0:16, 0, :], zx_sb[0:16, 4, :],
                                            zx_sb[0:16, 5, :], op=ALU.mult))
            b.wait(te, V, bcV)
            temm(pb[6][0:1, 0:T], onescol16[0:128, 0:1], hy2[:, 0, :],
                 start=True, stop=True)
            bcT = tinc()
            b.wait(ve, Tm, bcT)
            bc1V = b.inc(V, ve.tensor_copy(st2[0:1, 0:T], pb[6][0:1, 0:T]))
            b.wait(te, V, bc1V)
            temm(pb[6][:, T:2 * T], ones1p[:, 0:128], st2[0:128, 0:T],
                 start=True, stop=True)
            bc2T = tinc()
            b.wait(ve, Tm, bc2T)
            b.wait(ve, S, sp2)
            ve.tensor_tensor(hy1[32:36, 0, 0:HALF], hy1[32:36, 0, 0:HALF],
                             pb[6][32:36, T:T + HALF], op=ALU.mult)
            gfV = b.inc(V, ve.tensor_tensor(hy1[64:68, 0, HALF:T],
                                            hy1[64:68, 0, HALF:T],
                                            pb[6][64:68, T + HALF:2 * T],
                                            op=ALU.mult))
            b.wait(te, V, gfV)
            for c in range(2):
                temm(pb[2][:, c * HALF:(c + 1) * HALF],
                     e_hyd[0:128, c * 128:(c + 1) * 128],
                     hy1[0:128, 0, 0:HALF], start=True, stop=True)
                temm(pb[3][:, c * HALF:(c + 1) * HALF],
                     e_hyd2[0:128, c * 128:(c + 1) * 128],
                     hy1[0:128, 0, HALF:T], start=True, stop=True)
            geT2 = tinc()
            b.wait(ve, Tm, geT2)
            yV = 0
            for c in range(2):
                xs_c = zx_sb[:, 2 + c, :]
                ve.tensor_tensor(hy1[:, c, HALF:T],
                                 pb[2][:, c * HALF:(c + 1) * HALF],
                                 xs_c[:, 0:HALF], op=ALU.mult)
                yV = b.inc(V, ve.tensor_tensor(hy1[:, c, 0:HALF],
                                               pb[3][:, c * HALF:(c + 1) * HALF],
                                               xs_c[:, HALF:T], op=ALU.mult))
            b.wait(te, V, yV)
            for kc in range(2):
                temm(pb[7][32:36, 0:T], fcd_w[:, kc * 4:kc * 4 + 4],
                          zx_sb[:, 2 + kc, :], start=(kc == 0), stop=(kc == 1))
            fcT = tinc()
            b.wait(ve, Tm, fcT)
            fcV = b.inc(V, ve.tensor_scalar(bc_sb[32:36, 0:T], pb[7][32:36, 0:T],
                                            vecs[32:36, vix["hyD"]:vix["hyD"] + 1],
                                            None, op0=ALU.add))
            b.wait(te, V, fcV)
            for c in range(2):
                temm(pb[4][:, c * T:(c + 1) * T],
                     e_hyd[0:128, c * 128:(c + 1) * 128],
                     bc_sb[0:128, 0:T], start=True, stop=True)
            feT = tinc()
            b.wait(ve, Tm, feT)
            y2V = 0
            for c in range(2):
                ve.tensor_tensor(hy2[:, c, :], zx_sb[:, 2 + c, :],
                                 pb[4][:, c * T:(c + 1) * T], op=ALU.mult)
                y2V = b.inc(V, ve.tensor_tensor(hy1[:, c, :], hy1[:, c, :],
                                                hy2[:, c, :], op=ALU.add))
            b.wait(se, V, y2V)
            szS = 0
            for c in range(2):
                szS = b.inc(S, se.activation(hy2[:, c, :], zx_sb[:, c, :], AF.Silu))
            b.wait(ve, S, szS)
            yzV = 0
            for c in range(2):
                yzV = b.inc(V, ve.tensor_tensor(hy1[:, c, :], hy1[:, c, :],
                                                hy2[:, c, :], op=ALU.mult))
            b.wait(se, V, yzV)
            sqS = 0
            for c in range(2):
                sqS = b.inc(S, se.square(hy3[:, c, :], hy1[:, c, :]))
            b.wait(te, S, sqS)
            for c in range(2):
                temm(pb[5][0:1, 0:T], onescol[:, 0:1], hy3[:, c, :],
                          start=(c == 0), stop=(c == 1))
            msT = tinc()
            b.wait(ve, Tm, msT)
            vpeV = vsync(ve.tensor_scalar(st1[0:1, :], pb[5][0:1, 0:T],
                                          1.0 / 256.0, 1e-5, op0=ALU.mult,
                                          op1=ALU.add))
            b.wait(se, V, vpeV)
            rsS = b.inc(S, se.activation(st2[0:1, :], st1[0:1, :], AF.Sqrt))
            b.wait(ve, S, rsS)
            vsync(ve.reciprocal(st3[0:1, :], st2[0:1, :]))
            vsync(ve.tensor_tensor(st4[0:1, :], st3[0:1, :], st3[0:1, :],
                                   op=ALU.mult))
            vsync(ve.tensor_tensor(st4[0:1, :], st4[0:1, :], st1[0:1, :],
                                   op=ALU.mult))
            vsync(ve.tensor_scalar(st4[0:1, :], st4[0:1, :], -0.5, 1.5,
                                   op0=ALU.mult, op1=ALU.add))
            rdV = vsync(ve.tensor_tensor(st3[0:1, :], st3[0:1, :],
                                         st4[0:1, :], op=ALU.mult))
            b.wait(te, V, rdV)
            temm(pb[5][:, T:2 * T], ones1p[:, 0:128], st3[0:128, :],
                 start=True, stop=True)
            rbT = tinc()
            b.wait(ve, Tm, rbT)
            ynV = 0
            for c in range(2):
                nwv = vcol("nw0") if c == 0 else vcol("nw1")
                ynV = b.inc(V, ve.scalar_tensor_tensor(
                    hy1[:, c, :], hy1[:, c, :], nwv, pb[5][:, T:2 * T],
                    op0=ALU.mult, op1=ALU.mult))
            b.wait(te, V, ynV)
            for kc in range(2):
                temm(pb[2][:, 0:T], outw_w[:, kc * 128:kc * 128 + 128],
                          hy1[:, kc, :], start=(kc == 0), stop=(kc == 1))
            opT = tinc()
            b.wait(ve, Tm, opT)
            return b.inc(V, ve.tensor_copy(uout[:], pb[2][:, 0:T]))

        b.wait(ve, V, h0V)
        hy_done = hydra_pass(hc_sb, hd_sb)
        b.wait(ve, V, hy_done)
        hy_done = hydra_pass(hd_sb, hc_sb)
        b.wait(ve, V, hy_done)
        ve.tensor_scalar(hd_sb[:], hc_sb[:], alph1[:, 0:1], None, op0=ALU.mult)
        ve.scalar_tensor_tensor(hd_sb[:], h0_sb[:], alph[:, 0:1], hd_sb[:],
                                op0=ALU.mult, op1=ALU.add)
        h4V = b.inc(V, ve.tensor_tensor(hd_sb[:], hd_sb[:], h0_sb[:], op=ALU.add))

        def upevac(oc, pap):
            bupv = vecs[:, vix["bup0"] + oc: vix["bup0"] + oc + 1]
            ve.scalar_tensor_tensor(xbuf[:, oc, :], xbuf[:, oc, :], 2.0, pap,
                                    op0=ALU.mult, op1=ALU.add)
            return b.inc(V, ve.tensor_scalar(xbuf[:, oc, :], xbuf[:, oc, :],
                                             bupv, None, op0=ALU.add))
        if ot == 0:
            dump("hyd", dbgA=h0_sb[:], dbgB=hc_sb[:], dbgC=hd_sb[:])
        b.wait(te, V, h4V)
        stream_chain(dr["wup_s"], NCH, 1, lambda kc: hd_sb[:], upevac,
                     n_strip_cols=128)
        if ot == 0:
            dump("upx", dbgX=xbuf[:, :, :])
        ln3 = layer_norm(xbuf)
        if ot == 0:
            dump("ln3", dbgX=xbuf[:, :, :])

        if LIM < 6:
            b.wait(gp, V, b.cnt["V"])
            xstore_done = b.inc(G, gp.dma_start(xout[ot], xbuf[:, :, :]))
            continue
        b.wait(te, V, ln3)
        par = 0
        b.wait(sy, Tm, prev_group_end[par])
        sd = b.inc(W, sy.dma_start(wbuf[par][:, 0:NCH * 128], dr["ar_s"][:, :]))
        b.wait(te, W, sd)
        b.wait(te, V, prev_evac[par])
        for kc in range(NCH):
            temm(pb[par][0:128, 0:T], wbuf[par][:, kc * 128:kc * 128 + 128],
                      xv(kc), start=(kc == 0), stop=(kc == NCH - 1))
        prev_group_end[par] = tinc()
        b.wait(ve, Tm, prev_group_end[par])
        arV = b.inc(V, ve.tensor_copy(rh_sb[0:128, :], pb[par][0:128, 0:T]))
        prev_evac[par] = arV
        b.wait(te, V, arV)
        temm(pb[6][:, 0:T], dupc[0:128, :], rh_sb[0:128, :], start=True,
             stop=True)
        rdT = tinc()
        b.wait(ve, Tm, rdT)
        rhdV = b.inc(V, ve.tensor_copy(rhd_sb[:], pb[6][:, 0:T]))
        b.wait(te, V, arV)
        for blk in range(nblk):
            tetr(pb[7][0:blkw, blk * 16:blk * 16 + 16],
                         rh_sb[0:16, blk * blkw:(blk + 1) * blkw],
                         ident[0:16, 0:16])
        l1tT = tinc()
        b.wait(ve, Tm, l1tT)
        cfm_done = 0
        for blk in range(nblk):
            lgv = lg16[0:blkw, :]
            vsync(ve.tensor_copy(lgv, pb[7][0:blkw, blk * 16:blk * 16 + 16]))
            vsync(ve.tensor_reduce(t8a[0:blkw, 0:1], lgv, op=ALU.max, axis=AX.X))
            mxV = vsync(ve.tensor_scalar(t8b[0:blkw, 0:1], t8a[0:blkw, 0:1],
                                         -1.0, None, op0=ALU.mult))
            b.wait(se, V, mxV)
            exS = b.inc(S, se.activation(lgv, lgv, AF.Exp,
                                         bias=t8b[0:blkw, 0:1], scale=1.0))
            b.wait(ve, S, exS)
            vsync(ve.tensor_reduce(t8a[0:blkw, 0:1], lgv, op=ALU.add, axis=AX.X))
            vsync(ve.reciprocal(t8b[0:blkw, 0:1], t8a[0:blkw, 0:1]))
            vsync(ve.tensor_tensor(t8a[0:blkw, 1:2], t8a[0:blkw, 0:1],
                                   t8b[0:blkw, 0:1], op=ALU.mult))
            vsync(ve.tensor_scalar(t8a[0:blkw, 1:2], t8a[0:blkw, 1:2], -1.0, 2.0,
                                   op0=ALU.mult, op1=ALU.add))
            vsync(ve.tensor_tensor(t8b[0:blkw, 0:1], t8b[0:blkw, 0:1],
                                   t8a[0:blkw, 1:2], op=ALU.mult))
            vsync(ve.tensor_scalar(lgv, lgv, t8b[0:blkw, 0:1], None, op0=ALU.mult))
            vsync(ve.max(t8a[0:blkw, :], lgv))
            vsync(ve.scalar_tensor_tensor(p1_sb[0:blkw, 0:1], t8a[0:blkw, 0:1],
                                          1e-6, t8a[0:blkw, 1:2], op0=ALU.add,
                                          op1=ALU.add))
            vsync(ve.reciprocal(p1_sb[0:blkw, 1:2], p1_sb[0:blkw, 0:1]))
            vsync(ve.tensor_tensor(p1_sb[0:blkw, 2:3], p1_sb[0:blkw, 0:1],
                                   p1_sb[0:blkw, 1:2], op=ALU.mult))
            vsync(ve.tensor_scalar(p1_sb[0:blkw, 2:3], p1_sb[0:blkw, 2:3], -1.0,
                                   2.0, op0=ALU.mult, op1=ALU.add))
            vsync(ve.tensor_tensor(p1_sb[0:blkw, 1:2], p1_sb[0:blkw, 1:2],
                                   p1_sb[0:blkw, 2:3], op=ALU.mult))
            vsync(ve.tensor_tensor(p1_sb[0:blkw, 2:3], t8a[0:blkw, 0:1],
                                   p1_sb[0:blkw, 1:2], op=ALU.mult))
            vsync(ve.tensor_tensor(p1_sb[0:blkw, 3:4], t8a[0:blkw, 1:2],
                                   p1_sb[0:blkw, 1:2], op=ALU.mult))
            vsync(ve.tensor_scalar(gtm[0:blkw, 0:16], lgv, t8a[0:blkw, 0:1],
                                   p1_sb[0:blkw, 2:3], op0=ALU.is_equal,
                                   op1=ALU.mult))
            vsync(ve.tensor_scalar(ltm[0:blkw, 0:16], lgv, t8a[0:blkw, 1:2],
                                   p1_sb[0:blkw, 3:4], op0=ALU.is_equal,
                                   op1=ALU.mult))
            cV = vsync(ve.tensor_tensor(gtm[0:blkw, 0:16], gtm[0:blkw, 0:16],
                                        ltm[0:blkw, 0:16], op=ALU.add))
            b.wait(te, V, cV)
            tetr(pb[6][0:16, T + blk * blkw: T + (blk + 1) * blkw],
                         gtm[0:blkw, 0:16], ident[0:blkw, 0:blkw])
            ctT = tinc()
            b.wait(ve, Tm, ctT)
            cfm_done = b.inc(V, ve.tensor_copy(
                cfm[0:16, blk * blkw:(blk + 1) * blkw],
                pb[6][0:16, T + blk * blkw:T + (blk + 1) * blkw]))
        b.wait(te, V, max(cfm_done, rhdV))
        lastG = {2: 0, 3: 0}
        gVl = 0
        for c in range(8):
            bk = 2 + (c % 2)
            b.wait(te, V, lastG[bk])
            temm(pb[bk][:, 0:T], e_l1[0:16, c * 128:(c + 1) * 128],
                      cfm[0:16, 0:T], start=True, stop=True)
            ceT = tinc()
            b.wait(ve, Tm, ceT)
            gVl = b.inc(V, ve.tensor_tensor(bufB[:, c, :], pb[bk][:, 0:T],
                                            rhd_sb[:], op=ALU.mult))
            lastG[bk] = gVl

        def l1bevac(oc, pap):
            bbv = vecs[:, vix["l1bb0"] + oc: vix["l1bb0"] + oc + 1]
            return b.inc(V, ve.scalar_tensor_tensor(bufA[:, oc, :], pap, bbv,
                                                    xbuf[:, oc, :], op0=ALU.add,
                                                    op1=ALU.add))
        stream_chain(dr["l1b_s"], NCH, NCH, xv, l1bevac)

        def devac(oc, pap):
            return b.inc(V, ve.tensor_tensor(bufA[:, oc, :], bufA[:, oc, :], pap,
                                             op=ALU.add))
        b.wait(te, V, gVl)
        stream_chain(dr["bt_s"], NCH, 8, lambda kc: bufB[:, kc, :], devac,
                     n_strip_cols=1024)
        if ot == 0:
            dump("ffn1", dbgA=bufA[:, :, :], dbgB=cfm[0:16, :], dbgC=rhd_sb[:])
        cpV = b.inc(V, ve.tensor_copy(xbuf[:, 0:NCH, :], bufA[:, 0:NCH, :]))
        b.wait(te, V, cpV)
        ln4 = layer_norm(xbuf)

        b.wait(gp, V, ln4)
        xstore_done = b.inc(G, gp.dma_start(xout[ot], xbuf[:, :, :]))

    b.wait(gp, G, b.cnt["G"])
    es.close()
    return nc



def make_in_maps(P, g):
    Wd = prep_weights(P, g)
    vecs, vix = prep_vecs(P, g)
    Cd = prep_consts(g)
    shared = {}
    shared.update({k: np.ascontiguousarray(v) for k, v in Wd.items()})
    shared.update({k: np.ascontiguousarray(v) for k, v in Cd.items()})
    shared["vecs"] = np.ascontiguousarray(vecs)
    return shared, vecs, vix


def kernel(**inputs):
    g = Geo(D=4096, SL=512, T=256)
    P = {k: np.asarray(v, np.float32) for k, v in inputs.items()}
    shared, vecs, vix = make_in_maps(P, g)
    nc = build_full(g, vecs.shape[1], vix)
    x = P["L0_attn_hidden_states"]
    in_maps = []
    for core in range(NCORES):
        m = dict(shared)
        m["xin"] = prep_x(x, core, g)
        in_maps.append(m)
    res = run_bass_kernel_spmd(nc, in_maps, core_ids=list(range(NCORES)))
    outs = []
    for core in range(NCORES):
        outs.append(unprep_x(np.asarray(res.results[core]["xout"]), g))
    full = np.concatenate(outs, axis=1)
    return full.astype(np.float32)
